# version 02 (final)
# baseline (speedup 1.0000x reference)
import os
import numpy as np
import concourse.bass as bass
import concourse.mybir as mybir
from concourse.alu_op_type import AluOpType
from concourse.bass_utils import run_bass_kernel_spmd

F32 = mybir.dt.float32
BF16 = mybir.dt.bfloat16
I32 = mybir.dt.int32
U8 = mybir.dt.uint8
AF = mybir.ActivationFunctionType
AO = AluOpType
AX = mybir.AxisListType

FP64_SIM = bool(os.environ.get("BASS_SIM_FP64"))

QM = 32767.0
D = 1024
DH = 64
TC = 2048
TH = 1024
NI = 3072
EPS = 1e-5

ENGINES = ("sync", "vector", "scalar", "gpsimd", "tensor")


class Ctr:
    __slots__ = ("sem", "val", "eng", "is_dma")
    def __init__(self, sem):
        self.sem = sem; self.val = 0; self.eng = None; self.is_dma = False
    def cp(self):
        return (self, self.val)


class Op:
    __slots__ = ("eng", "fn", "waits", "inc", "tag")
    def __init__(self, eng, fn, waits, inc, tag):
        self.eng = eng; self.fn = fn; self.waits = waits; self.inc = inc; self.tag = tag


class Sched:
    def __init__(self, nc):
        self.nc = nc
        self.ops = []

    def ctr(self, name):
        return Ctr(self.nc.semaphore(name).__enter__())

    def add(self, eng, fn, waits=(), inc=None, inc_n=1, tag=""):
        ws = []
        for w in waits:
            if w is None:
                continue
            if isinstance(w, Ctr):
                ws.append((w, w.val))
            else:
                c, v = w
                assert v <= c.val, f"wait on future value {v} > {c.val} ({tag})"
                assert not c.is_dma or v == c.val, f"partial wait on DMA ctr {v}<{c.val} ({tag})"
                if v > 0:
                    ws.append((c, v))
        self.ops.append(Op(eng, fn, ws, (inc, inc_n) if inc is not None else None, tag))
        if inc is not None:
            if inc.eng is None:
                inc.eng = eng
            assert inc.eng == eng, f"ctr inc from {eng} and {inc.eng} ({tag})"
            if inc_n >= 16:
                inc.is_dma = True
            inc.val += inc_n

    def emit(self):
        with self.nc.Block() as block:
            for eng_name in ENGINES:
                ops = [o for o in self.ops if o.eng == eng_name]
                if not ops:
                    continue
                def body(eng, ops=ops, eng_name=eng_name):
                    seen = {}
                    for o in ops:
                        for (c, v) in o.waits:
                            if seen.get(id(c), -1) < v:
                                eng.wait_ge(c.sem, v)
                                seen[id(c)] = v
                        instr = o.fn(eng)
                        if o.inc is not None:
                            c, n = o.inc
                            assert instr is not None, f"op {o.tag} has no instr"
                            instr.then_inc(c.sem, n)
                getattr(block, eng_name)(body)
        self.ops = []


def build(dumps=(), max_ops=None):
    dumps = set(dumps)
    landmarks = {}
    nc = bass.Bass("TRN2", target_bir_lowering=False)
    s = Sched(nc)
    A = s.add

    x_in = nc.dram_tensor("x", [TC, D], F32, kind="ExternalInput").ap()
    w_in = {nm: nc.dram_tensor("w" + nm, [D, D], F32, kind="ExternalInput").ap()
            for nm in ("q", "k", "v", "g", "o")}
    gamma_in = nc.dram_tensor("gamma", [1, D], F32, kind="ExternalInput").ap()
    beta_in = nc.dram_tensor("beta", [1, D], F32, kind="ExternalInput").ap()
    wup_in = nc.dram_tensor("wup_s", [NI // 8, D], F32, kind="ExternalInput").ap()
    wgate_in = nc.dram_tensor("wgate_s", [NI // 8, D], F32, kind="ExternalInput").ap()
    wdown_in = nc.dram_tensor("wdown_s", [D // 8, NI], F32, kind="ExternalInput").ap()
    psel_in = nc.dram_tensor("psel", [128, 8], F32, kind="ExternalInput").ap()
    out = nc.dram_tensor("out", [TC, D], F32, kind="ExternalOutput").ap()

    dump_aps = {}
    def dmp(name, shape, dt=F32):
        dump_aps[name] = nc.dram_tensor("dump_" + name, shape, dt, kind="ExternalOutput").ap()
        return dump_aps[name]

    dr = lambda name, shape, dt: nc.dram_tensor(name, shape, dt, kind="Internal")
    wT_d = {nm: dr(f"w{nm}T_d", [D, D], BF16) for nm in ("q", "k", "v", "g", "o")}
    x2_d = dr("x2_d", [TC, D], F32)
    xln3_d = dr("xln3_d", [TC, D], BF16)
    ag_st_in = dr("ag_state_in", [8 * 128, DH], F32)
    ag_st_out = dr("ag_state_out", [64 * 128, DH], F32)
    ag_up_in = dr("ag_up_in", [D, NI // 8], BF16)
    ag_up_out = dr("ag_up_out", [8 * D, NI // 8], BF16)
    ag_gate_in = dr("ag_gate_in", [D, NI // 8], BF16)
    ag_gate_out = dr("ag_gate_out", [8 * D, NI // 8], BF16)
    ag_down_in = dr("ag_down_in", [NI, D // 8], BF16)
    ag_down_out = dr("ag_down_out", [8 * NI, D // 8], BF16)
    up_d = dr("up_d", [NI, TC], F32)
    gate_d = dr("gate_d", [NI, TC], F32)
    p_d = dr("p_d", [NI, TC], F32)
    o_d = dr("o_d", [NI, TC], F32)
    mxb_d = dr("mxb_d", [1, 256], F32)
    ar_io = {k: (dr(f"ar_in_{k}", [1, 2], F32), dr(f"ar_out_{k}", [1, 2], F32))
             for k in ("x", "ug", "p", "o")}

    sb = lambda name, shape, dt=F32: nc.sbuf_tensor(name, shape, dt).__enter__()

    ident = sb("ident", [128, 128], BF16)
    mask2 = sb("mask2", [128, 128])
    iscr = sb("iscr", [128, 128])
    iota_r = sb("iota_r", [128, 128])
    iota_c = sb("iota_c", [128, 128])
    onesb = sb("onesb", [1, 128], BF16)
    onesf = sb("onesf", [1, 128])

    xlt = [sb(f"xlt{i}", [128, D], BF16) for i in range(2)]
    ptile = [sb(f"ptile{j}", [128, 128], BF16) for j in range(2)]
    ktmE = [[sb(f"ktm{j}_{i}", [128, 128], BF16) for i in range(8)] for j in range(2)]
    s_fp = [sb(f"sfp{g}", [128, DH]) for g in range(8)]
    s_bf = [sb(f"sbf{g}", [128, DH], BF16) for g in range(8)]
    s_scr = sb("s_scr", [128, DH])
    bc = [sb(f"bc{g}", [128, 16]) for g in range(8)]
    dcum1 = [sb(f"dcum1{g}", [128, 1]) for g in range(8)]
    qsaveP = [sb(f"qsaveP{g}", [128, 128], BF16) for g in range(8)]
    qcorr = sb("qcorr", [128, 64], BF16)
    betc = sb("betc", [128, 1], BF16)
    sin_bf = [sb(f"sin{g}", [128, DH], BF16) for g in range(8)]
    sin_pp = [sb(f"sinpp{j}", [128, DH]) for j in range(2)]
    agbuf = [sb(f"agbuf{j}", [128, DH]) for j in range(2)]
    psel_sb = sb("psel_sb", [128, 8])

    mxcols = sb("mxcols", [128, 384])
    mxrow = sb("mxrow", [1, 256])
    mxone = sb("mxone", [1, 2])
    gmax = sb("gmax", [128, 8])
    gscr = sb("gscr", [128, 2])
    bn6 = sb("bn6", [128, 12])
    bn2 = sb("bn2", [128, 2])
    rstd = sb("rstd", [128, 1])
    lbias = sb("lbias", [128, 1])
    eps_sb = sb("eps_sb", [128, 1])
    gam_sb = sb("gam_sb", [128, 8])
    bet_sb = sb("bet_sb", [128, 8])
    bwo = sb("bwo", [1, D], BF16)
    ln_scr = sb("ln_scr", [128, D])
    x2t = sb("x2t", [128, D])
    xl3t = [sb(f"xl3t{i}", [128, D], BF16) for i in range(2)]
    xqtile = sb("xqtile", [128, D], BF16)
    xi32 = sb("xi32", [128, D], I32)
    oqi = sb("oqi", [128, 128], I32)
    fjunk = sb("fjunk", [128, 512], BF16)
    fout = [sb(f"fout{j}", [128, 512], F32) for j in range(4)]
    wbuf = [sb(f"wbuf{k}", [128, 128], BF16) for k in range(24)]
    wev = sb("wev", [128, 512], BF16)

    ARENA = 147456
    arena = sb("arena", [128, ARENA], U8)
    arena_addr = nc.lookup_mloc(arena).addr

    build.arena_addr = arena_addr

    def ov(name, shape, dt, off):
        per_part = int(np.prod(shape[1:])) * mybir.dt.size(dt)
        assert off + per_part <= ARENA, f"overlay {name} overflows arena"
        return nc.alloc_sbuf_tensor_at(name, shape, dt, offset=arena_addr + off)

    SEG_YT = 0
    SEG_WV = 32768
    SEG_XLNT = 49152
    SEG_VT = 65536
    SEG_QK = 81920
    SEG_EXP = 90112
    SEG_XT = 106496
    SEG_XQ = 114688

    yt = [ov(f"yt{i}", [128, D], F32, SEG_YT + i * 4096) for i in range(8)]
    wv_sb = [ov(f"wv{k}", [128, D], BF16, SEG_WV + k * 2048) for k in range(8)]
    wdT = ([ov(f"wdT{i}", [128, D], BF16, SEG_YT + i * 2048) for i in range(16)]
           + [ov(f"wdT{16 + i}", [128, D], BF16, SEG_WV + i * 2048) for i in range(8)])
    xlnT = [ov(f"xlnT{g}", [128, TH], BF16, SEG_XLNT + g * 2048) for g in range(8)]
    fload = [ov(f"fload{j}", [128, TC], F32, SEG_XLNT + j * 8192) for j in range(2)]
    vt = [ov(f"vt{i}", [128, D], BF16, SEG_VT + i * 2048) for i in range(8)]
    fi32 = ov("fi32", [128, TC], I32, SEG_VT)
    fm32 = ov("fm32", [128, TC], I32, SEG_VT + 8192)
    qE = [ov(f"qE{j}", [128, TH], BF16, SEG_QK + j * 2048) for j in range(2)]
    kE = [ov(f"kE{j}", [128, TH], BF16, SEG_QK + 4096 + j * 2048) for j in range(2)]
    fsig = ov("fsig", [128, TC], F32, SEG_QK)
    cumE = [ov(f"cum{j}", [128, TH], F32, SEG_EXP + j * 4096) for j in range(2)]
    expM = [ov(f"expM{j}", [128, TH], F32, SEG_EXP + j * 4096) for j in range(2)]
    logaE = [ov(f"loga{j}", [128, TH], F32, SEG_EXP + 8192 + j * 4096) for j in range(2)]
    expP = [ov(f"expP{j}", [128, TH], F32, SEG_EXP + 8192 + j * 4096) for j in range(2)]
    fbf = ov("fbf", [128, TC], F32, SEG_EXP)
    oqc = [ov(f"oq{i}", [128, 128], BF16, SEG_EXP + 8192 + i * 256) for i in range(24)]
    xt = [ov(f"xt{i}", [128, D], F32, SEG_XT + i * 4096) for i in range(2)]
    wp_f = ov("wp_f", [128, NI], F32, SEG_XQ)
    wp_s = ov("wp_s", [128, 96], F32, SEG_XQ + 12288)
    wp_r = ov("wp_r", [128, 96], F32, SEG_XQ + 12672)
    wp_i = ov("wp_i", [128, NI], I32, SEG_XQ + 13056)
    wp_q = ov("wp_q", [128, NI], BF16, SEG_XQ + 25344)
    y0save = ov("y0save", [128, D], F32, SEG_XQ)
    ylt = ov("ylt", [128, D], BF16, SEG_XQ + 4096)
    ylnT = [ov(f"ylnT{k}", [128, 128], BF16, SEG_XQ + 6144 + k * 256) for k in range(8)]
    wo_raw = ov("wo_raw", [128, D], BF16, SEG_XQ + 8192)
    xqt = [ov(f"xqt{g}", [128, TC], BF16, SEG_XQ + g * 4096) for g in range(8)]
    wp_h = ov("wp_h", [128, NI], F32, SEG_YT) if FP64_SIM else None
    wp_m = ov("wp_m", [128, NI], F32, SEG_YT + 12288) if FP64_SIM else None
    rh1k = ov("rh1k", [128, D], F32, SEG_XT) if FP64_SIM else None
    rh2k = ov("rh2k", [128, TC], F32, SEG_XT) if FP64_SIM else None

    psA = nc.psum_tensor("psA", [128, 1024], F32).__enter__()
    psB = nc.psum_tensor("psB", [128, 1024], F32).__enter__()
    psT = nc.psum_tensor("psT", [128, 512], BF16).__enter__()
    psS = nc.psum_tensor("psS", [128, 512], F32).__enter__()
    psY = nc.psum_tensor("psY", [128, 512], F32).__enter__()
    psK = nc.psum_tensor("psK", [128, 512], F32).__enter__()

    dpx = s.ctr("dpx")
    dxt = [s.ctr("dxtA"), s.ctr("dxtB")]
    dx3 = [s.ctr("dx3A"), s.ctr("dx3B")]
    dwv = s.ctr("dwv")
    dwor = s.ctr("dwor")
    dwd = s.ctr("dwd")
    dbn = s.ctr("dbn")
    dwA = s.ctr("dwA")
    dwB = s.ctr("dwB")
    dagb = [s.ctr("dagA"), s.ctr("dagB")]
    dscf = [s.ctr("dscA"), s.ctr("dscB")]
    dst = s.ctr("dst")
    ds_st = s.ctr("ds_st")
    ds2 = s.ctr("ds2")
    ds3 = [s.ctr("ds3A"), s.ctr("ds3B")]
    ds_fo = [s.ctr(f"dsfo{k}") for k in range(4)]
    ds_fbf = s.ctr("ds_fbf")
    ds_out = s.ctr("ds_out")
    dsxl = [s.ctr("dsxlA"), s.ctr("dsxlB")]
    dsy = s.ctr("dsy")
    dsq = s.ctr("dsq")
    dsc = s.ctr("dsc")
    dcc = s.ctr("dcc")
    ccu = s.ctr("ccu")
    ccg = s.ctr("ccg")
    ccd = s.ctr("ccd")
    ccs = s.ctr("ccs")
    cv = s.ctr("cv")
    ca = s.ctr("ca")
    ct = s.ctr("ct")
    cg = s.ctr("cg")

    RG = [list(range(8))]
    last = {}
    lw = last.get

    def ln_stats_apply(src_ap, dst_ap, waits=(), tag=""):
        A("vector", lambda e: e.bn_stats(bn6[:, 0:6], src_ap[:, 0:512]),
          waits=list(waits) + [lw("lnscr")], inc=cv, tag=tag + "bns")
        A("vector", lambda e: e.bn_stats(bn6[:, 6:12], src_ap[:, 512:1024]), waits=[cv.cp()], inc=cv, tag=tag + "bns2")
        A("vector", lambda e: e.bn_aggr(bn2[:, :], bn6[:, :]), waits=[cv.cp()], inc=cv, tag=tag + "bna")
        A("scalar", lambda e: e.activation(rstd[:, :], bn2[:, 1:2], AF.Ln, bias=eps_sb[:, :]), waits=[cv.cp()], inc=ca, tag=tag + "l")
        A("scalar", lambda e: e.activation(rstd[:, :], rstd[:, :], AF.Exp, scale=-0.5), waits=[ca.cp()], inc=ca, tag=tag + "e")
        A("vector", lambda e: e.tensor_scalar(lbias[:, :], bn2[:, 0:1], rstd[:, :], -1.0, AO.mult, AO.mult),
          waits=[ca.cp()], inc=cv, tag=tag + "b")
        A("scalar", lambda e: e.activation(dst_ap, src_ap, AF.Identity, bias=lbias[:, :], scale=rstd[:, :]),
          waits=[cv.cp()], inc=ca, tag=tag + "a")
        last["lnscr"] = ca.cp()
        return ca.cp()

    def round_ts(out_ap, in_ap, alpha, waits, tag="", scr=None):
        if not FP64_SIM:
            A("vector", lambda e: e.tensor_scalar(out_ap, in_ap, alpha, None, AO.mult),
              waits=waits, inc=cv, tag=tag)
        else:
            A("vector", lambda e: e.tensor_scalar(scr, in_ap, 0.0, -0.5, AO.is_ge, AO.add),
              waits=list(waits) + [cv.cp()], inc=cv, tag=tag + "h")
            A("vector", lambda e: e.scalar_tensor_tensor(out_ap, in_ap, alpha, scr, AO.mult, AO.add),
              waits=[cv.cp()], inc=cv, tag=tag)
        return cv.cp()

    A("gpsimd", lambda e: e.iota(iota_r[:, :], [[0, 128]], channel_multiplier=1,
                                 allow_small_or_imprecise_dtypes=True), inc=cg)
    A("gpsimd", lambda e: e.iota(iota_c[:, :], [[1, 128]], channel_multiplier=0,
                                 allow_small_or_imprecise_dtypes=True), inc=cg)
    A("vector", lambda e: e.tensor_tensor(ident[:, :], iota_r[:, :], iota_c[:, :], AO.is_equal),
      waits=[cg], inc=cv, tag="ident")
    A("gpsimd", lambda e: e.iota(iscr[:, :], [[0, 2], [1, 64]], channel_multiplier=0,
                                 allow_small_or_imprecise_dtypes=True), inc=cg, tag="m2i")
    A("vector", lambda e: e.tensor_scalar(mask2[0:64, :], iota_r[0:64, :], 0.0, None, AO.add),
      waits=[cg], inc=cv, tag="m2a")
    A("vector", lambda e: e.tensor_scalar(mask2[64:128, :], iota_r[64:128, :], -64.0, None, AO.add),
      waits=[cv.cp()], inc=cv, tag="m2b")
    A("vector", lambda e: e.tensor_tensor(mask2[:, :], mask2[:, :], iscr[:, :], AO.is_le),
      waits=[cv.cp()], inc=cv, tag="m2c")
    A("vector", lambda e: e.memset(onesb[:, :], 1.0), inc=cv)
    A("vector", lambda e: e.memset(eps_sb[:, :], EPS), inc=cv)
    A("vector", lambda e: e.memset(mxone[:, :], 0.0), inc=cv, tag="mx0")
    A("vector", lambda e: e.memset(gmax[:, :], 0.0), inc=cv, tag="gm0")
    A("vector", lambda e: e.memset(onesf[:, :], 1.0), inc=cv)
    A("sync", lambda e: e.dma_start(out=psel_sb[:, :], in_=psel_in), inc=dpx, inc_n=16)
    A("gpsimd", lambda e: e.dma_start(out=gam_sb[:, :], in_=gamma_in.rearrange("o (a p) -> (o p) a", p=128),
                                      allow_slow_non_contiguous=True), inc=dst, inc_n=16, tag="ldgam")
    A("gpsimd", lambda e: e.dma_start(out=bet_sb[:, :], in_=beta_in.rearrange("o (a p) -> (o p) a", p=128),
                                      allow_slow_non_contiguous=True), inc=dst, inc_n=16, tag="ldbet")
    consts_cv = cv.cp()

    for nm in ("q", "k", "g", "v", "o"):
        for et in range(8):
            A("gpsimd", lambda e, nm=nm, et=et: e.dma_start(out=wp_q[:, 0:D],
                                                            in_=w_in[nm][et * 128:(et + 1) * 128, :]),
              waits=[dst, ct.cp(), cv.cp()], inc=dst, inc_n=16, tag=f"w{nm}ld")
            for pair in range(4):
                for jj in range(2):
                    dti = pair * 2 + jj
                    A("tensor", lambda e, dti=dti, jj=jj: e.transpose(
                        psT[:, jj * 128:jj * 128 + 128], wp_q[:, dti * 128:dti * 128 + 128], ident[:, :]),
                      waits=[dst, consts_cv, cv.cp()], inc=ct, tag=f"w{nm}tr")
                A("vector", lambda e: e.tensor_copy(wev[:, 0:256], psT[:, 0:256]),
                  waits=[ct.cp(), (dst, dst.val)], inc=cv, tag=f"w{nm}ev")
                for jj in range(2):
                    dti = pair * 2 + jj
                    A("gpsimd", lambda e, nm=nm, dti=dti, et=et, jj=jj: e.dma_start(
                        out=wT_d[nm].ap()[dti * 128:dti * 128 + 128, et * 128:et * 128 + 128],
                        in_=wev[:, jj * 128:jj * 128 + 128]),
                      waits=[cv.cp()], inc=dst, inc_n=16, tag=f"w{nm}sp")

    def prep_ffn_shard(wi, nrows, nblk, ag_in, tag):
        cols = nblk * 32
        for rb in range(nrows // 128):
            A("gpsimd", lambda e, rb=rb: e.dma_start(out=wp_f[:, 0:cols], in_=wi[rb * 128:(rb + 1) * 128, :]),
              waits=[dst, cv.cp(), ct.cp()], inc=dst, inc_n=16, tag=f"{tag}ld")
            A("vector", lambda e: e.tensor_reduce(wp_s[:, 0:nblk],
                                                  wp_f[:, 0:cols].rearrange("p (a b) -> p a b", b=32),
                                                  AX.X, AO.max, apply_absolute_value=True),
              waits=[dst], inc=cv, tag=f"{tag}mx")
            A("vector", lambda e: e.tensor_scalar(wp_s[:, 0:nblk], wp_s[:, 0:nblk], 7e-12, None, AO.max),
              waits=[cv.cp()], inc=cv, tag=f"{tag}fl")
            A("vector", lambda e: e.reciprocal(wp_r[:, 0:nblk], wp_s[:, 0:nblk]), waits=[cv.cp()], inc=cv)
            A("vector", lambda e: e.tensor_scalar(wp_r[:, 0:nblk], wp_r[:, 0:nblk], 7.0, None, AO.mult),
              waits=[cv.cp()], inc=cv, tag=f"{tag}r7")
            rbc = bass.AP(wp_r.ap().tensor, 0, [[96, 128], [1, nblk], [0, 32]])
            sbc = bass.AP(wp_s.ap().tensor, 0, [[96, 128], [1, nblk], [0, 32]])
            grid = lambda t: t[:, 0:cols].rearrange("p (a b) -> p a b", b=32)
            if not FP64_SIM:
                A("vector", lambda e: e.tensor_tensor(grid(wp_i), grid(wp_f), rbc, AO.mult),
                  waits=[cv.cp()], inc=cv, tag=f"{tag}rd")
            else:
                A("vector", lambda e: e.tensor_scalar(wp_h[:, 0:cols], wp_f[:, 0:cols], 0.0, -0.5,
                                                      AO.is_ge, AO.add), waits=[cv.cp()], inc=cv)
                A("vector", lambda e: e.tensor_tensor(grid(wp_m), grid(wp_f), rbc, AO.mult),
                  waits=[cv.cp()], inc=cv)
                A("vector", lambda e: e.tensor_tensor(wp_i[:, 0:cols], wp_m[:, 0:cols], wp_h[:, 0:cols],
                                                      AO.add), waits=[cv.cp()], inc=cv, tag=f"{tag}rd")
            A("vector", lambda e: e.reciprocal(wp_s[:, 0:nblk], wp_r[:, 0:nblk]), waits=[cv.cp()], inc=cv)
            A("vector", lambda e: e.tensor_tensor(grid(wp_q), grid(wp_i), sbc, AO.mult),
              waits=[cv.cp()], inc=cv, tag=f"{tag}dq")
            for dt_ in range(cols // 128):
                A("tensor", lambda e, dt_=dt_: e.transpose(psT[:, 0:128], wp_q[:, dt_ * 128:dt_ * 128 + 128],
                                                           ident[:, :]),
                  waits=[cv.cp(), ct.cp(), consts_cv], inc=ct, tag=f"{tag}tr")
                A("vector", lambda e: e.tensor_copy(wev[:, 0:128], psT[:, 0:128]),
                  waits=[ct.cp(), (dst, dst.val)], inc=cv)
                A("gpsimd", lambda e, dt_=dt_, rb=rb: e.dma_start(
                    out=ag_in.ap()[dt_ * 128:dt_ * 128 + 128, rb * 128:rb * 128 + 128],
                    in_=wev[:, 0:128]), waits=[cv.cp()], inc=dst, inc_n=16, tag=f"{tag}wr")

    prep_ffn_shard(wup_in, NI // 8, 32, ag_up_in, "fu")
    prep_ffn_shard(wgate_in, NI // 8, 32, ag_gate_in, "fg")
    prep_ffn_shard(wdown_in, D // 8, 96, ag_down_in, "fd")

    A("gpsimd", lambda e: e.collective_compute("AllGather", AO.bypass, replica_groups=RG,
                                               ins=[ag_up_in.ap().opt()], outs=[ag_up_out.ap().opt()]),
      waits=[dst], inc=ccu, tag="ag_up")
    A("gpsimd", lambda e: e.collective_compute("AllGather", AO.bypass, replica_groups=RG,
                                               ins=[ag_gate_in.ap().opt()], outs=[ag_gate_out.ap().opt()]),
      inc=ccg, tag="ag_gate")
    A("gpsimd", lambda e: e.collective_compute("AllGather", AO.bypass, replica_groups=RG,
                                               ins=[ag_down_in.ap().opt()], outs=[ag_down_out.ap().opt()]),
      inc=ccd, tag="ag_down")
    landmarks["prep"] = len(s.ops)
    wp_done_ct = ct.cp()
    wp_done_cv = cv.cp()
    for k in ("ylt", "ylnT", "wo_raw", "y0w"):
        last[k] = wp_done_ct

    for g in range(8):
        A("vector", lambda e, g=g: e.memset(s_fp[g][:, :], 0.0), inc=cv)
        A("vector", lambda e, g=g: e.memset(s_bf[g][:, :], 0.0), inc=cv)

    y_dump = dmp("y", [TC, D]) if "y" in dumps else None
    x2_dump = dmp("x2", [TC, D]) if "x2" in dumps else None
    xln_dump = dmp("xln", [TC, D], BF16) if "xln" in dumps else None
    q0_dump = dmp("q0", [128, TH], BF16) if "q0" in dumps else None
    v0_dump = dmp("v0", [128, D], BF16) if "v0" in dumps else None
    gm_dump = dmp("gm", [128, 8]) if "gm" in dumps else None
    up_dump = dmp("up", [NI, TC]) if "up" in dumps else None
    gt_dump = dmp("gt", [NI, TC]) if "gt" in dumps else None
    pd_dump = dmp("pd", [NI, TC]) if "pd" in dumps else None
    od_dump = dmp("od", [NI, TC]) if "od" in dumps else None
    mxi = [0]
    fencebuf = sb("fencebuf", [1, 4])

    def dump_fence(c):
        A("vector", lambda e: e.memset(fencebuf[:, :], 0.0), waits=[(c, c.val)], inc=cv, tag="dfence")
        return cv.cp()

    def ypost(half, tau, y_src, ywait):
        t0g = half * 8 + tau
        xr = xt[t0g % 2]
        A("sync", lambda e, t0g=t0g, xr=xr: e.dma_start(out=xr[:, :], in_=x_in[t0g * 128:(t0g + 1) * 128, :]),
          waits=[lw(f"xt{t0g % 2}")], inc=dxt[t0g % 2], inc_n=16, tag=f"yx{t0g}")
        xw = dxt[t0g % 2].cp()
        ck = ln_stats_apply(y_src[:, :], ylt[:, :], waits=[ywait, lw("ylt")], tag=f"gb{t0g}")
        for pair in range(4):
            for jj in range(2):
                kc = pair * 2 + jj
                A("tensor", lambda e, kc=kc, jj=jj: e.transpose(psT[:, jj * 128:jj * 128 + 128],
                                                                ylt[:, kc * 128:kc * 128 + 128], ident[:, :]),
                  waits=[ck, lw("ylnT"), ct.cp(), cv.cp()], inc=ct, tag=f"ytr{t0g}")
            for jj in range(2):
                kc = pair * 2 + jj
                A("vector", lambda e, kc=kc, jj=jj: e.tensor_copy(ylnT[kc][:, :], psT[:, jj * 128:jj * 128 + 128]),
                  waits=[ct.cp()], inc=cv, tag=f"yev{t0g}")
        last["ylt"] = cv.cp()
        yT_ck = cv.cp()
        for eg in range(2):
            for kc in range(8):
                A("tensor", lambda e, eg=eg, kc=kc: e.matmul(
                    psA[:, eg * 512:(eg + 1) * 512], ylnT[kc][:, :], wv_sb[kc][:, eg * 512:(eg + 1) * 512],
                    start=(kc == 0), stop=False),
                  waits=[yT_ck, lw("woT"), lw("psA"), lw("bwo")], inc=ct, tag=f"wo{t0g}")
            A("tensor", lambda e, eg=eg: e.matmul(psA[:, eg * 512:(eg + 1) * 512], onesb[0:1, :],
                                                  bwo[0:1, eg * 512:(eg + 1) * 512], start=False, stop=True),
              inc=ct, tag=f"wob{t0g}")
        last["ylnT"] = ct.cp()
        mm_ck = ct.cp()
        ck2 = ln_stats_apply(psA[:, :], ln_scr[:, :], waits=[mm_ck, lw("ln_scr")], tag=f"l2_{t0g}")
        last["psA"] = ck2
        A("vector", lambda e, xr=xr: e.tensor_tensor(x2t[:, :], ln_scr[:, :], xr[:, :], AO.add),
          waits=[ck2, xw, lw("x2t"), (ds2, ds2.val)], inc=cv, tag=f"res{t0g}")
        last["ln_scr"] = cv.cp()
        last[f"xt{t0g % 2}"] = cv.cp()
        x2_ck = cv.cp()
        A("gpsimd", lambda e, t0g=t0g: e.dma_start(out=x2_d.ap()[t0g * 128:(t0g + 1) * 128, :], in_=x2t[:, :]),
          waits=[x2_ck], inc=ds2, inc_n=16, tag=f"x2sp{t0g}")
        if x2_dump is not None:
            A("gpsimd", lambda e, t0g=t0g: e.dma_start(out=x2_dump[t0g * 128:(t0g + 1) * 128, :], in_=x2t[:, :]),
              waits=[x2_ck], inc=ds2, inc_n=16)
        x3 = xl3t[t0g % 2]
        ck3 = ln_stats_apply(x2t[:, :], x3[:, :], waits=[x2_ck, lw(f"x3w{t0g % 2}")], tag=f"l3_{t0g}")
        last["x2t"] = ck3
        mi = mxi[0]; mxi[0] += 4
        for q, (sl, sc) in enumerate((((0, 512), 1.0), ((512, 1024), 1.0), ((0, 512), -1.0), ((512, 1024), -1.0))):
            A("vector", lambda e, mi=mi, q=q, sl=sl, sc=sc, x3=x3: e.tensor_scalar(
                fjunk[:, :], x3[:, sl[0]:sl[1]], sc, None, AO.mult, AO.max,
                accum_out=mxcols[:, mi + q:mi + q + 1]),
              waits=([ck3, lw("fjunk")] if q == 0 else [cv.cp()]), inc=cv, tag=f"mx{t0g}_{q}")
        last["fjunk"] = cv.cp()
        A("gpsimd", lambda e, t0g=t0g, x3=x3: e.dma_start(out=xln3_d.ap()[t0g * 128:(t0g + 1) * 128, :],
                                                          in_=x3[:, :]),
          waits=[cv.cp()], inc=ds3[t0g % 2], inc_n=16, tag=f"x3sp{t0g}")
        last[f"x3w{t0g % 2}"] = ds3[t0g % 2]

    ag_st_cc = None
    for half in range(2):
        for i in range(8):
            xr = xt[i % 2]
            A("sync", lambda e, i=i, half=half, xr=xr: e.dma_start(
                out=xr[:, :], in_=x_in[half * TH + i * 128: half * TH + (i + 1) * 128, :]),
              waits=[lw(f"xt{i % 2}")], inc=dxt[i % 2], inc_n=16, tag=f"ldx{half}_{i}")
            xw = dxt[i % 2].cp()
            buf = xlt[i % 2]
            ck = ln_stats_apply(xr[:, :], buf[:, :], waits=[xw, lw(f"xlt{i % 2}")], tag=f"l1_{half}_{i}")
            last[f"xt{i % 2}"] = ck
            if xln_dump is not None:
                A("gpsimd", lambda e, i=i, half=half, buf=buf: e.dma_start(
                    out=xln_dump[half * TH + i * 128: half * TH + (i + 1) * 128, :], in_=buf[:, :]),
                  waits=[ck], inc=dsxl[i % 2], inc_n=16)
                last[f"xlt{i % 2}"] = dump_fence(dsxl[i % 2])
            for pair in range(4):
                for jj in range(2):
                    kc = pair * 2 + jj
                    A("tensor", lambda e, kc=kc, jj=jj, buf=buf: e.transpose(
                        psT[:, jj * 128:jj * 128 + 128], buf[:, kc * 128:kc * 128 + 128], ident[:, :]),
                      waits=[ck, ct.cp(), cv.cp()], inc=ct, tag=f"xtr{half}_{i}")
                for jj in range(2):
                    kc = pair * 2 + jj
                    A("vector", lambda e, kc=kc, jj=jj, i=i: e.tensor_copy(
                        xlnT[kc][:, i * 128:(i + 1) * 128], psT[:, jj * 128:jj * 128 + 128]),
                      waits=[ct.cp()], inc=cv, tag=f"xev{half}_{i}")
            last[f"xlt{i % 2}"] = cv.cp()
        xlnT_ck = cv.cp()

        for kc in range(8):
            A("sync", lambda e, kc=kc: e.dma_start(out=wv_sb[kc][:, :],
                                                   in_=wT_d["v"].ap()[kc * 128:(kc + 1) * 128, :]),
              waits=[(dst, dst.val), lw("wv_res")], inc=dwv, inc_n=16, tag=f"ldwv{half}")
        wv_ck = dwv.cp()
        for tt in range(8):
            pbank, pw = (psA, "psA") if tt % 2 == 0 else (psB, "psB")
            for kc in range(8):
                for vg in range(2):
                    A("tensor", lambda e, tt=tt, kc=kc, vg=vg, pbank=pbank: e.matmul(
                        pbank[:, vg * 512:(vg + 1) * 512], xlnT[kc][:, tt * 128:(tt + 1) * 128],
                        wv_sb[kc][:, vg * 512:(vg + 1) * 512],
                        start=(kc == 0), stop=(kc == 7)),
                      waits=[xlnT_ck, wv_ck, lw(pw)], inc=ct, tag=f"v{half}_{tt}")
            A("scalar", lambda e, tt=tt, pbank=pbank: e.activation(vt[tt][:, :], pbank[:, :], AF.Copy),
              waits=[ct.cp(), lw(f"vt{tt}")], inc=ca, tag=f"vev{half}_{tt}")
            last[pw] = ca.cp()
            last[f"vt{tt}"] = ca.cp()
        if v0_dump is not None and half == 0:
            A("gpsimd", lambda e: e.dma_start(out=v0_dump, in_=vt[0][:, :]), waits=[lw("vt0")], inc=dsq, inc_n=16)
            last["vt0"] = dump_fence(dsq)
        last["wv_res"] = ct.cp()

        for E in range(8):
            j = E % 2
            dw = dwA if j == 0 else dwB
            for pi, nm in enumerate(("g", "q", "k")):
                for kc in range(8):
                    slot = pi * 8 + kc
                    A("sync", lambda e, nm=nm, kc=kc, E=E, slot=slot: e.dma_start(
                        out=wbuf[slot][:, :], in_=wT_d[nm].ap()[kc * 128:(kc + 1) * 128, E * 128:(E + 1) * 128]),
                      waits=[lw("wslot"), (dst, dst.val)], inc=dw, inc_n=16, tag=f"wld{nm}{E}")
            wck = (dw, dw.val)

            pz, pzn = (psA, "psA") if j == 0 else (psB, "psB")
            for tg in range(2):
                for kc in range(8):
                    A("tensor", lambda e, tg=tg, kc=kc, pz=pz: e.matmul(
                        pz[:, tg * 512:(tg + 1) * 512], wbuf[0 * 8 + kc][:, :],
                        xlnT[kc][:, tg * 512:(tg + 1) * 512], start=(kc == 0), stop=(kc == 7)),
                      waits=[wck, xlnT_ck, lw(pzn)], inc=ct, tag=f"z{half}_{E}")
            zmm = ct.cp()
            A("scalar", lambda e, pz=pz, j=j: e.activation(expP[j][:, :], pz[:, :], AF.Exp),
              waits=[zmm, lw(f"expP{j}")], inc=ca, tag=f"ez{E}")
            A("scalar", lambda e, j=j: e.activation(expM[j][:, :], expP[j][:, :], AF.Ln, bias=1.0),
              waits=[ca.cp(), lw(f"expM{j}")], inc=ca, tag=f"l1p{E}")
            A("vector", lambda e, pz=pz, j=j: e.tensor_tensor(logaE[j][:, :], pz[:, :], expM[j][:, :], AO.subtract),
              waits=[ca.cp()], inc=cv, tag=f"lg{E}")
            last[pzn] = cv.cp()
            for c in range(16):
                A("vector", lambda e, c=c, j=j: e.tensor_tensor_scan(
                    cumE[j][:, c * 64:(c + 1) * 64], logaE[j][:, c * 64:(c + 1) * 64],
                    logaE[j][:, c * 64:(c + 1) * 64], 0.0, AO.add, AO.bypass),
                  waits=[cv.cp()], inc=cv, tag=f"sc{E}_{c}")
            scan_ck = cv.cp()
            A("scalar", lambda e, j=j: e.activation(expP[j][:, :], cumE[j][:, :], AF.Exp),
              waits=[scan_ck], inc=ca, tag=f"eP{E}")
            eP_ck = ca.cp()
            A("vector", lambda e, E=E, j=j: e.tensor_copy(
                bc[E][:, :], bass.AP(expP[j].ap().tensor, 63, [[TH, 128], [64, 16]])),
              waits=[eP_ck, lw(f"bcr{E}")], inc=cv, tag=f"bc{E}")
            bc_ck = cv.cp()
            if half == 0:
                A("vector", lambda e, E=E: e.tensor_copy(dcum1[E][:, :], bc[E][:, 0:1]),
                  waits=[bc_ck], inc=cv, tag=f"dc1{E}")
            A("scalar", lambda e, j=j: e.activation(expM[j][:, :], cumE[j][:, :], AF.Exp, scale=-1.0),
              waits=[scan_ck, eP_ck], inc=ca, tag=f"eM{E}")
            exp_ck = ca.cp()

            pq, pqn = (psB, "psB") if j == 0 else (psA, "psA")
            for tg in range(2):
                for kc in range(8):
                    A("tensor", lambda e, tg=tg, kc=kc, pq=pq: e.matmul(
                        pq[:, tg * 512:(tg + 1) * 512], wbuf[1 * 8 + kc][:, :],
                        xlnT[kc][:, tg * 512:(tg + 1) * 512], start=(kc == 0), stop=(kc == 7)),
                      waits=[wck, xlnT_ck, lw(pqn)], inc=ct, tag=f"q{half}_{E}")
            qmm = ct.cp()
            A("vector", lambda e, pq=pq, j=j: e.tensor_tensor(qE[j][:, :], pq[:, :], expP[j][:, :], AO.mult),
              waits=[qmm, eP_ck, lw(f"qE{j}"), lw(f"qE{j}d")], inc=cv, tag=f"qs{E}")
            q_ck = cv.cp()
            last[pqn] = q_ck
            last[f"expP{j}"] = q_ck
            if q0_dump is not None and half == 0 and E == 0:
                A("gpsimd", lambda e: e.dma_start(out=q0_dump, in_=qE[0][:, :]), waits=[q_ck], inc=dsq, inc_n=16)
                last[f"qE{j}d"] = dump_fence(dsq)

            pk, pkn = (psA, "psA") if j == 0 else (psB, "psB")
            for tg in range(2):
                for kc in range(8):
                    A("tensor", lambda e, tg=tg, kc=kc, pk=pk: e.matmul(
                        pk[:, tg * 512:(tg + 1) * 512], wbuf[2 * 8 + kc][:, :],
                        xlnT[kc][:, tg * 512:(tg + 1) * 512], start=(kc == 0), stop=(kc == 7)),
                      waits=[wck, xlnT_ck, lw(pkn)], inc=ct, tag=f"k{half}_{E}")
            kmm = ct.cp()
            last["wslot"] = kmm
            A("vector", lambda e, pk=pk, j=j: e.tensor_tensor(kE[j][:, :], pk[:, :], expM[j][:, :], AO.mult),
              waits=[kmm, exp_ck, lw(f"kE{j}")], inc=cv, tag=f"ks{E}")
            k_ck = cv.cp()
            last[pkn] = k_ck
            last[f"expM{j}"] = k_ck
            last[f"cum{j}"] = k_ck

            for tt in range(8):
                A("tensor", lambda e, tt=tt, j=j: e.transpose(psT[:, 0:128], kE[j][:, tt * 128:(tt + 1) * 128],
                                                         ident[:, :]),
                  waits=[k_ck, cv.cp(), ct.cp()], inc=ct, tag=f"ktr{E}_{tt}")
                A("vector", lambda e, tt=tt, j=j: e.tensor_copy(ktmE[j][tt][:, :], psT[:, 0:128]),
                  waits=[ct.cp()], inc=cv, tag=f"ktev{E}_{tt}")
            ktm_ck = cv.cp()

            if half == 0:
                A("vector", lambda e, E=E, j=j: e.tensor_copy(qsaveP[E][:, :], qE[j][:, 0:128]),
                  waits=[q_ck], inc=cv, tag=f"qsv{E}")

            for c in range(16):
                beta = (c % 2) * 64
                gam = 64 - beta
                tau = c // 2
                t0 = c * 64
                jj = c % 2
                A("tensor", lambda e, beta=beta, t0=t0, j=j: e.matmul(
                    psS[beta:beta + 64, 0:64], kE[j][0:64, t0:t0 + 64], qE[j][0:64, t0:t0 + 64],
                    start=True, stop=True, tile_position=(0, beta)),
                  waits=[q_ck, k_ck, lw("psS")], inc=ct, tag=f"s{E}_{c}a")
                A("tensor", lambda e, gam=gam, t0=t0, j=j: e.matmul(
                    psS[gam:gam + 64, 64:128], kE[j][64:128, t0:t0 + 64], qE[j][64:128, t0:t0 + 64],
                    start=True, stop=True, tile_position=(64, gam)),
                  inc=ct, tag=f"s{E}_{c}b")
                sc_ck = ct.cp()
                A("vector", lambda e, beta=beta, jj=jj: e.tensor_tensor(
                    ptile[jj][beta:beta + 64, 0:64], psS[beta:beta + 64, 0:64],
                    mask2[beta:beta + 64, 0:64], AO.mult),
                  waits=[sc_ck, consts_cv, lw(f"pt{jj}")], inc=cv, tag=f"m{E}_{c}")
                A("vector", lambda e, beta=beta, gam=gam, jj=jj: e.tensor_tensor(
                    ptile[jj][beta:beta + 64, 64:128], psS[gam:gam + 64, 64:128],
                    mask2[gam:gam + 64, 64:128], AO.mult),
                  waits=[sc_ck], inc=cv, tag=f"m{E}_{c}h")
                last["psS"] = cv.cp()
                mk_ck = cv.cp()
                A("tensor", lambda e, beta=beta, tau=tau, E=E, jj=jj: e.matmul(
                    psY[beta:beta + 64, 0:64], ptile[jj][beta:beta + 64, 0:64],
                    vt[tau][beta:beta + 64, E * 128:E * 128 + 64],
                    start=True, stop=False, tile_position=(beta, beta)),
                  waits=[mk_ck, lw("psY"), lw("sbf_w")], inc=ct, tag=f"p{E}_{c}a")
                A("tensor", lambda e, beta=beta, t0=t0, E=E, j=j: e.matmul(
                    psY[beta:beta + 64, 0:64], qE[j][0:64, t0:t0 + 64], s_bf[E][0:64, :],
                    start=False, stop=True, tile_position=(0, beta)),
                  waits=([ct.cp()] if beta == 64 else []), inc=ct, tag=f"y{E}_{c}a")
                A("tensor", lambda e, beta=beta, tau=tau, E=E, jj=jj: e.matmul(
                    psY[beta:beta + 64, 64:128], ptile[jj][beta:beta + 64, 64:128],
                    vt[tau][beta:beta + 64, E * 128 + 64:E * 128 + 128],
                    start=True, stop=False, tile_position=(beta, beta)),
                  waits=([ct.cp()] if beta == 64 else []), inc=ct, tag=f"p{E}_{c}b")
                A("tensor", lambda e, beta=beta, t0=t0, E=E, j=j: e.matmul(
                    psY[beta:beta + 64, 64:128], qE[j][64:128, t0:t0 + 64], s_bf[E][64:128, :],
                    start=False, stop=True, tile_position=(64, beta)),
                  waits=([ct.cp()] if beta == 0 else []), inc=ct, tag=f"y{E}_{c}b")
                yi_ck = ct.cp()
                last[f"pt{jj}"] = yi_ck
                A("tensor", lambda e, beta=beta, tau=tau, E=E, j=j: e.matmul(
                    psK[0:64, 0:64], ktmE[j][tau][beta:beta + 64, 0:64],
                    vt[tau][beta:beta + 64, E * 128:E * 128 + 64],
                    start=True, stop=True, tile_position=(beta, 0)),
                  waits=[ktm_ck, lw("psK")], inc=ct, tag=f"kv{E}_{c}a")
                A("tensor", lambda e, beta=beta, tau=tau, E=E, j=j: e.matmul(
                    psK[64:128, 0:64], ktmE[j][tau][beta:beta + 64, 64:128],
                    vt[tau][beta:beta + 64, E * 128 + 64:E * 128 + 128],
                    start=True, stop=True, tile_position=(beta, 64)),
                  inc=ct, tag=f"kv{E}_{c}b")
                kv_ck = ct.cp()
                A("vector", lambda e, E=E: e.tensor_tensor(s_scr[:, :], s_fp[E][:, :], psK[0:128, 0:64], AO.add),
                  waits=[kv_ck], inc=cv, tag=f"u{E}_{c}a")
                last["psK"] = cv.cp()
                A("vector", lambda e, E=E, c=c: e.tensor_scalar(s_fp[E][:, :], s_scr[:, :], bc[E][:, c:c + 1],
                                                                None, AO.mult),
                  waits=[bc_ck, cv.cp()], inc=cv, tag=f"u{E}_{c}b")
                A("vector", lambda e, E=E: e.tensor_copy(s_bf[E][:, :], s_fp[E][:, :]),
                  waits=[cv.cp()], inc=cv, tag=f"u{E}_{c}c")
                last["sbf_w"] = cv.cp()
                A("vector", lambda e, beta=beta, tau=tau, E=E: e.tensor_copy(
                    yt[tau][beta:beta + 64, E * 128:E * 128 + 128], psY[beta:beta + 64, 0:128]),
                  waits=[yi_ck, lw(f"ytw{tau}")] + ([wp_done_cv] if FP64_SIM else []), inc=cv, tag=f"e{E}_{c}")
                last["psY"] = cv.cp()
            last[f"qE{j}"] = ct.cp()
            last[f"kE{j}"] = ct.cp()
            last[f"bcr{E}"] = cv.cp()
        mix_ck = cv.cp()

        if half == 0:
            A("vector", lambda e: e.tensor_copy(y0save[:, :], yt[0][:, :]), waits=[mix_ck, wp_done_ct], inc=cv, tag="y0sv")
            last["ytw0"] = cv.cp()
            last["y0w"] = cv.cp()

        if y_dump is not None:
            for tau in range(8):
                ysrc = y0save if (half == 0 and tau == 0) else yt[tau]
                A("gpsimd", lambda e, tau=tau, half=half, ysrc=ysrc: e.dma_start(
                    out=y_dump[half * TH + tau * 128: half * TH + (tau + 1) * 128, :], in_=ysrc[:, :]),
                  waits=[mix_ck, cv.cp()], inc=dsy, inc_n=16)
            fck = dump_fence(dsy)
            for tau in range(8):
                last[f"ytw{tau}"] = fck
            last["y0w"] = fck

        if half == 1:
            for g in range(8):
                A("gpsimd", lambda e, g=g: e.dma_start(out=ag_st_in.ap()[g * 128:(g + 1) * 128, :],
                                                       in_=s_fp[g][:, :]),
                  waits=[mix_ck], inc=ds_st, inc_n=16, tag=f"sto{g}")
            A("gpsimd", lambda e: e.collective_compute("AllGather", AO.bypass, replica_groups=RG,
                                                       ins=[ag_st_in.ap().opt()], outs=[ag_st_out.ap().opt()]),
              waits=[(ds_st, 128)], inc=ccs, tag="ag_state")
            ag_st_cc = (ccs, 1)

        for kc in range(8):
            A("sync", lambda e, kc=kc: e.dma_start(out=wv_sb[kc][:, :],
                                                   in_=wT_d["o"].ap()[kc * 128:(kc + 1) * 128, :]),
              waits=[lw("wv_res"), (dst, dst.val)], inc=dwv, inc_n=16, tag=f"ldwo{half}")
        wo_ld = dwv.cp()
        for kc in range(8):
            A("vector", lambda e, kc=kc: e.tensor_scalar(wv_sb[kc][:, :], wv_sb[kc][:, :],
                                                         gam_sb[:, kc:kc + 1], None, AO.mult),
              waits=[wo_ld, (dst, dst.val)], inc=cv, tag=f"gf{half}_{kc}")
        last["woT"] = cv.cp()
        if half == 0:
            for eg in range(2):
                for kc in range(8):
                    A("sync", lambda e, kc=kc: e.dma_start(out=wo_raw[:, :],
                                                           in_=wT_d["o"].ap()[kc * 128:(kc + 1) * 128, :]),
                      waits=[lw("worawm"), lw("wo_raw"), (dst, dst.val)], inc=dwor, inc_n=16, tag="ldwor")
                    wor = dwor.cp()
                    last["wo_raw"] = None
                    A("vector", lambda e, kc=kc: e.tensor_copy(betc[:, :], bet_sb[:, kc:kc + 1]),
                      waits=[wor, lw("betcm"), (dst, dst.val)], inc=cv, tag="betc")
                    bck = cv.cp()
                    A("tensor", lambda e, kc=kc, eg=eg: e.matmul(
                        psS[0:1, 0:512], betc[:, :], wo_raw[:, eg * 512:(eg + 1) * 512],
                        start=(kc == 0), stop=(kc == 7)),
                      waits=[bck, lw("psS")], inc=ct, tag="bwom")
                    last["worawm"] = ct.cp()
                    last["betcm"] = ct.cp()
                A("vector", lambda e, eg=eg: e.tensor_copy(bwo[0:1, eg * 512:(eg + 1) * 512], psS[0:1, 0:512]),
                  waits=[ct.cp()], inc=cv, tag="bwoev")
                last["psS"] = cv.cp()
            last["bwo"] = cv.cp()

        landmarks[f"mix{half}"] = len(s.ops)
        taus = list(range(8)) if half == 1 else list(range(1, 8))
        for tau in taus:
            ypost(half, tau, yt[tau], mix_ck)
            last[f"ytw{tau}"] = cv.cp()
        last["wv_res"] = ct.cp()
        landmarks[f"half{half}"] = len(s.ops)

    for g in range(8):
        A("vector", lambda e: e.memset(sin_pp[0][:, :], 0.0), waits=[lw("sinpp0")], inc=cv, tag=f"si0{g}")
        for r in range(8):
            A("sync", lambda e, r=r, g=g: e.dma_start(out=agbuf[r % 2][:, :],
                                                      in_=ag_st_out.ap()[r * 1024 + g * 128: r * 1024 + (g + 1) * 128, :]),
              waits=[ag_st_cc, lw(f"agb{r % 2}")], inc=dagb[r % 2], inc_n=16, tag=f"agl{g}_{r}")
            A("vector", lambda e, r=r: e.scalar_tensor_tensor(
                sin_pp[(r + 1) % 2][:, :], agbuf[r % 2][:, :], psel_sb[:, r:r + 1], sin_pp[r % 2][:, :],
                AO.mult, AO.add),
              waits=[dagb[r % 2].cp(), (dpx, 16), cv.cp()], inc=cv, tag=f"sia{g}_{r}")
            last[f"agb{r % 2}"] = cv.cp()
        A("vector", lambda e, g=g: e.tensor_copy(sin_bf[g][:, :], sin_pp[0][:, :]), waits=[cv.cp()], inc=cv,
          tag=f"sib{g}")
        last["sinpp0"] = cv.cp()
    sin_ck = cv.cp()

    for g in range(8):
        for cc in range(2):
            if cc == 0:
                qsl = lambda a, b, g=g: qsaveP[g][a:b, 0:64]
                qck = sin_ck
            else:
                A("vector", lambda e, g=g: e.tensor_scalar(qcorr[:, :], qsaveP[g][:, 64:128],
                                                           dcum1[g][:, :], None, AO.mult),
                  waits=[sin_ck, lw("qcm")], inc=cv, tag=f"qd{g}")
                qsl = lambda a, b: qcorr[a:b, :]
                qck = cv.cp()
            A("tensor", lambda e, g=g, qsl=qsl: e.matmul(
                psS[0:64, 128:192], qsl(0, 64), sin_bf[g][0:64, :],
                start=True, stop=True, tile_position=(0, 0)),
              waits=[qck, sin_ck, lw("psS")], inc=ct, tag=f"c{g}_{cc}a")
            A("tensor", lambda e, g=g, qsl=qsl: e.matmul(
                psS[64:128, 192:256], qsl(64, 128), sin_bf[g][64:128, :],
                start=True, stop=True, tile_position=(64, 64)),
              inc=ct, tag=f"c{g}_{cc}b")
            cmm = ct.cp()
            last["qcm"] = cmm
            A("vector", lambda e, g=g, cc=cc: e.tensor_tensor(
                y0save[cc * 64:(cc + 1) * 64, g * 128:g * 128 + 64],
                y0save[cc * 64:(cc + 1) * 64, g * 128:g * 128 + 64],
                psS[0:64, 128:192], AO.add),
              waits=[cmm, lw("y0w")], inc=cv, tag=f"ca{g}_{cc}")
            A("vector", lambda e, g=g, cc=cc: e.tensor_tensor(
                y0save[cc * 64:(cc + 1) * 64, g * 128 + 64:g * 128 + 128],
                y0save[cc * 64:(cc + 1) * 64, g * 128 + 64:g * 128 + 128],
                psS[64:128, 192:256], AO.add),
              waits=[cmm], inc=cv, tag=f"ca{g}_{cc}h")
            last["psS"] = cv.cp()
            last["y0w"] = cv.cp()
    corr_ck = cv.cp()
    if y_dump is not None:
        A("gpsimd", lambda e: e.dma_start(out=y_dump[0:128, :], in_=y0save[:, :]),
          waits=[corr_ck], inc=dsy, inc_n=16)
        last["y0w"] = dump_fence(dsy)
    ypost(0, 0, y0save, corr_ck)

    landmarks["s4"] = len(s.ops)
    A("vector", lambda e: e.memset(gscr[:, 0:1], 0.0), waits=[ct.cp(), ca.cp(), cg.cp()], inc=cv, tag="attn_end")
    attn_end = cv.cp()
    for k in ("fl0", "fl1", "fi32w", "fm32w", "fsig", "fbf", "wdt_g", "oqc_g", "xqt_g"):
        last[k] = attn_end

    def local_finalize(groups, tag):
        n = len(groups)
        for i, (a, b) in enumerate(groups):
            A("vector", lambda e, i=i, a=a, b=b: e.tensor_reduce(gscr[:, i:i + 1], mxcols[:, a:b], AX.X, AO.max),
              waits=[cv.cp()], inc=cv, tag=tag + f"r{i}")
        A("sync", lambda e, n=n: e.dma_start(out=mxb_d.ap().rearrange("a (p v) -> (a p) v", v=n)[0:128, :],
                                             in_=gscr[:, 0:n]),
          waits=[cv], inc=dbn, inc_n=16, tag=tag + "bo")
        A("sync", lambda e, n=n: e.dma_start(out=mxrow[0:1, 0:128 * n], in_=mxb_d.ap()[0:1, 0:128 * n]),
          waits=[(dbn, dbn.val)], inc=dbn, inc_n=16, tag=tag + "bi")
        for v in range(n):
            A("vector", lambda e, v=v, n=n: e.tensor_reduce(
                mxone[0:1, v:v + 1], bass.AP(mxrow.ap().tensor, v, [[256, 1], [n, 128]]),
                AX.X, AO.max), waits=[(dbn, dbn.val), lw("mxone")], inc=cv, tag=tag + f"f{v}")
        last["mxone"] = cv.cp()
        return cv.cp()

    def ar_and_bcast(key, cols, tag):
        ain, aout = ar_io[key]
        n = len(cols)
        A("sync", lambda e, ain=ain: e.dma_start(out=ain.ap()[0:1, 0:2], in_=mxone[0:1, 0:2]),
          waits=[cv], inc=dbn, inc_n=16, tag=tag + "ai")
        A("gpsimd", lambda e, ain=ain, aout=aout: e.collective_compute(
            "AllReduce", AO.max, replica_groups=RG, ins=[ain.ap().opt()], outs=[aout.ap().opt()]),
          waits=[(dbn, dbn.val)], inc=dcc, tag=tag + "ar")
        A("sync", lambda e, aout=aout: e.dma_start(out=mxone[0:1, 0:2], in_=aout.ap()[0:1, 0:2]),
          waits=[(dcc, dcc.val), lw("mxone")], inc=dbn, inc_n=16, tag=tag + "ao")
        A("tensor", lambda e, n=n: e.matmul(psS[:, 128:128 + n], onesf[0:1, :], mxone[0:1, 0:n],
                                            start=True, stop=True),
          waits=[(dbn, dbn.val), lw("psS"), consts_cv], inc=ct, tag=tag + "bc")
        for i, oc in enumerate(cols):
            A("vector", lambda e, i=i, oc=oc: e.tensor_copy(gmax[:, oc:oc + 1], psS[:, 128 + i:128 + i + 1]),
              waits=[ct.cp()], inc=cv, tag=tag + "g")
        last["psS"] = cv.cp()
        last["mxone"] = cv.cp()
        return cv.cp()

    def recip_scale(col_in, col_out, mult, waits, tag):
        A("vector", lambda e: e.tensor_scalar(gscr[:, 0:1], gmax[:, col_in:col_in + 1], QM * 1e-12, None, AO.max),
          waits=list(waits), inc=cv, tag=tag + "m")
        A("vector", lambda e: e.reciprocal(gscr[:, 1:2], gscr[:, 0:1]), waits=[cv.cp()], inc=cv, tag=tag + "r")
        A("vector", lambda e: e.tensor_scalar(gmax[:, col_out:col_out + 1], gscr[:, 1:2], mult, None, AO.mult),
          waits=[cv.cp()], inc=cv, tag=tag + "s")
        return cv.cp()

    local_finalize([(0, 64)], "r2")
    r2 = ar_and_bcast("x", [0], "r2")
    A("vector", lambda e: e.tensor_copy(gmax[:, 6:7], gmax[:, 0:1]), waits=[r2], inc=cv, tag="gmxraw")
    gmx_ck = cv.cp()
    ax_ck = recip_scale(0, 0, QM, [gmx_ck], "ax")

    landmarks["r2"] = len(s.ops)
    for i in range(16):
        x3 = xl3t[i % 2]
        A("sync", lambda e, i=i, x3=x3: e.dma_start(out=x3[:, :], in_=xln3_d.ap()[i * 128:(i + 1) * 128, :]),
          waits=[(ds3[0], ds3[0].val), (ds3[1], ds3[1].val), lw(f"x3r{i % 2}")], inc=dx3[i % 2], inc_n=16,
          tag=f"x3l{i}")
        rck = round_ts(xi32[:, :], x3[:, :], gmax[:, 0:1], [dx3[i % 2].cp(), ax_ck, lw("xi32")],
                       tag=f"xr{i}", scr=rh1k[:, :] if FP64_SIM else None)
        last[f"x3r{i % 2}"] = rck
        A("vector", lambda e: e.tensor_copy(xqtile[:, :], xi32[:, :]), waits=[rck, lw("xqtile")],
          inc=cv, tag=f"xq{i}")
        last["xi32"] = cv.cp()
        xq_ck = cv.cp()
        for pair in range(4):
            for jj in range(2):
                kc = pair * 2 + jj
                A("tensor", lambda e, kc=kc, jj=jj: e.transpose(
                    psT[:, jj * 128:jj * 128 + 128], xqtile[:, kc * 128:kc * 128 + 128], ident[:, :]),
                  waits=[xq_ck, ct.cp(), cv.cp()], inc=ct, tag=f"xqt{i}")
            for jj in range(2):
                kc = pair * 2 + jj
                A("vector", lambda e, i=i, kc=kc, jj=jj: e.tensor_copy(
                    xqt[kc][:, i * 128:(i + 1) * 128], psT[:, jj * 128:jj * 128 + 128]),
                  waits=[ct.cp(), lw("xqt_g") if i == 0 else None], inc=cv, tag=f"xqe{i}")
        last["xqtile"] = ct.cp()
    xqT_ck = cv.cp()

    landmarks["xq"] = len(s.ops)
    mxi[0] = 0
    fo_i = [0]
    for it in range(24):
        for wi, (ag_o, spill) in enumerate(((ag_up_out, up_d), (ag_gate_out, gate_d))):
            dw = dwA if (it * 2 + wi) % 2 == 0 else dwB
            r, lc = it // 3, it % 3
            for kc in range(8):
                A("sync", lambda e, kc=kc, r=r, lc=lc, ag_o=ag_o: e.dma_start(
                    out=wbuf[kc][:, :],
                    in_=ag_o.ap()[r * D + kc * 128: r * D + (kc + 1) * 128, lc * 128:(lc + 1) * 128]),
                  waits=[lw("wslot"), (ccu, 1), (ccg, 1)], inc=dw, inc_n=16, tag=f"f1w{it}_{wi}")
            f1w = (dw, dw.val)
            for tgh in range(2):
                pb, pbn = (psA, "psA") if tgh == 0 else (psB, "psB")
                for kc in range(8):
                    for t2 in range(2):
                        A("tensor", lambda e, kc=kc, t2=t2, tgh=tgh, pb=pb: e.matmul(
                            pb[:, t2 * 512:(t2 + 1) * 512], wbuf[kc][:, :],
                            xqt[kc][:, tgh * 1024 + t2 * 512: tgh * 1024 + (t2 + 1) * 512],
                            start=(kc == 0), stop=(kc == 7)),
                          waits=[f1w, xqT_ck, lw(pbn)], inc=ct, tag=f"f1m{it}_{wi}_{tgh}")
                f1m = ct.cp()
                for t2 in range(2):
                    fo = fout[fo_i[0] % 4]; fo_i[0] += 1
                    mi = mxi[0]; mxi[0] += 2
                    A("vector", lambda e, t2=t2, pb=pb, fo=fo, mi=mi: e.tensor_scalar(
                        fo[:, :], pb[:, t2 * 512:(t2 + 1) * 512], 1.0, None, AO.mult,
                        AO.max, accum_out=mxcols[:, mi:mi + 1]),
                      waits=[f1m, lw(f"fo{id(fo)}")], inc=cv, tag=f"f1e{it}_{wi}_{tgh}_{t2}")
                    A("vector", lambda e, fo=fo, mi=mi: e.tensor_scalar(
                        fjunk[:, :], fo[:, :], -1.0, None, AO.mult, AO.max,
                        accum_out=mxcols[:, mi + 1:mi + 2]),
                      waits=[cv.cp(), lw("fjunk")], inc=cv, tag=f"f1n{it}_{wi}_{tgh}_{t2}")
                    last["fjunk"] = cv.cp()
                    foc = ds_fo[(fo_i[0] - 1) % 4]
                    A("gpsimd", lambda e, it=it, tgh=tgh, t2=t2, fo=fo, spill=spill: e.dma_start(
                        out=spill.ap()[it * 128:(it + 1) * 128, tgh * 1024 + t2 * 512: tgh * 1024 + (t2 + 1) * 512],
                        in_=fo[:, :]),
                      waits=[cv.cp()], inc=foc, inc_n=16, tag=f"f1s{it}_{wi}_{tgh}_{t2}")
                    last[f"fo{id(fo)}"] = foc
                last[pbn] = cv.cp()
            last["wslot"] = ct.cp()

    landmarks["f1"] = len(s.ops)
    def strided_max(dstcol, off, tag):
        A("vector", lambda e, dstcol=dstcol, off=off: e.tensor_reduce(
            gscr[:, dstcol:dstcol + 1],
            bass.AP(mxcols.ap().tensor, off, [[384, 128], [16, 24], [1, 8]]),
            AX.XY, AO.max), waits=[cv.cp()], inc=cv, tag=tag)
    strided_max(0, 0, "ugu")
    strided_max(1, 8, "ugg")
    A("sync", lambda e: e.dma_start(out=mxb_d.ap().rearrange("a (p v) -> (a p) v", v=2)[0:128, :], in_=gscr[:, 0:2]),
      waits=[cv], inc=dbn, inc_n=16, tag="ugbo")
    A("sync", lambda e: e.dma_start(out=mxrow[0:1, 0:256], in_=mxb_d.ap()[0:1, 0:256]),
      waits=[(dbn, dbn.val)], inc=dbn, inc_n=16, tag="ugbi")
    for v in range(2):
        A("vector", lambda e, v=v: e.tensor_reduce(
            mxone[0:1, v:v + 1], bass.AP(mxrow.ap().tensor, v, [[256, 1], [2, 128]]),
            AX.X, AO.max), waits=[(dbn, dbn.val), lw("mxone")], inc=cv, tag=f"ugf{v}")
    last["mxone"] = cv.cp()
    r3 = ar_and_bcast("ug", [1, 2], "r3")
    A("vector", lambda e: e.tensor_tensor(gmax[:, 3:4], gmax[:, 2:3], gmax[:, 6:7], AO.mult),
      waits=[r3, gmx_ck], inc=cv, tag="sg1")
    A("vector", lambda e: e.tensor_scalar(gmax[:, 3:4], gmax[:, 3:4], 1.0 / (QM * QM), None, AO.mult),
      waits=[cv.cp()], inc=cv, tag="sg")
    sg_ck = cv.cp()
    au_ck = recip_scale(1, 1, QM, [r3], "au")
    ag_ck = recip_scale(2, 2, QM, [r3, sg_ck], "ag")

    mxi[0] = 0
    for it in range(24):
        jl = it % 2
        A("scalar", lambda e, it=it, jl=jl: e.dma_start(out=fload[jl][:, :],
                                                        in_=gate_d.ap()[it * 128:(it + 1) * 128, :]),
          waits=[lw(f"fl{jl}")] + [(c, c.val) for c in ds_fo], inc=dscf[jl], inc_n=16, tag=f"f2l{it}")
        rck = round_ts(fi32[:, :], fload[jl][:, :], gmax[:, 2:3], [dscf[jl].cp(), ag_ck, lw("fi32w"), lw("fi32")],
                       tag=f"f2r{it}", scr=rh2k[:, :] if FP64_SIM else None)
        A("scalar", lambda e: e.activation(fsig[:, :], fi32[:, :], AF.Sigmoid, scale=gmax[:, 3:4]),
          waits=[rck, sg_ck, lw("fsig")], inc=ca, tag=f"f2s{it}")
        sck = ca.cp()
        if not FP64_SIM:
            A("vector", lambda e: e.tensor_scalar(fm32[:, :], fsig[:, :], 65536.0, 65535.0, AO.mult, AO.min),
              waits=[sck, lw("fm32w"), lw("fm32")], inc=cv, tag=f"f2m{it}")
        else:
            A("vector", lambda e: e.tensor_scalar(rh2k[:, :], fsig[:, :], 65536.0, 0.5, AO.mult, AO.add),
              waits=[sck, cv.cp()], inc=cv, tag=f"f2m{it}a")
            A("vector", lambda e: e.tensor_scalar(fm32[:, :], rh2k[:, :], 65535.0, None, AO.min),
              waits=[cv.cp(), lw("fm32w"), lw("fm32")], inc=cv, tag=f"f2m{it}")
        mck = cv.cp()
        last["fsig"] = mck
        A("vector", lambda e: e.tensor_tensor(fbf[:, :], fi32[:, :], fm32[:, :], AO.mult),
          waits=[mck, lw("fbf")], inc=cv, tag=f"f2p{it}")
        pck = cv.cp()
        last["fi32"] = pck
        last["fm32"] = pck
        mi = mxi[0]; mxi[0] += 1
        A("vector", lambda e, mi=mi: e.tensor_reduce(mxcols[:, mi:mi + 1], fbf[:, :], AX.X, AO.max,
                                                     apply_absolute_value=True),
          waits=[pck], inc=cv, tag=f"f2x{it}")
        A("gpsimd", lambda e, it=it: e.dma_start(out=p_d.ap()[it * 128:(it + 1) * 128, :], in_=fbf[:, :]),
          waits=[cv.cp()], inc=ds_fbf, inc_n=16, tag=f"f2w{it}")
        last["fbf"] = ds_fbf
        last[f"fl{jl}"] = pck
    local_finalize([(0, 24)], "r4x")
    r4 = ar_and_bcast("p", [4], "r4")
    ap_ck = recip_scale(4, 4, QM, [r4], "ap")

    landmarks["f2"] = len(s.ops)
    mxi[0] = 0
    for it in range(24):
        A("scalar", lambda e, it=it: e.dma_start(out=fload[0][:, :], in_=p_d.ap()[it * 128:(it + 1) * 128, :]),
          waits=[lw("fl0"), (ds_fbf, ds_fbf.val) if it == 0 else None], inc=dsc, inc_n=16, tag=f"f3lp{it}")
        A("scalar", lambda e, it=it: e.dma_start(out=fload[1][:, :], in_=up_d.ap()[it * 128:(it + 1) * 128, :]),
          waits=[lw("fl1")], inc=dsc, inc_n=16, tag=f"f3lu{it}")
        f3l = (dsc, dsc.val)
        rck1 = round_ts(fi32[:, :], fload[0][:, :], gmax[:, 4:5], [f3l, ap_ck, lw("fi32")],
                        tag=f"f3r{it}", scr=rh2k[:, :] if FP64_SIM else None)
        rck2 = round_ts(fm32[:, :], fload[1][:, :], gmax[:, 1:2], [f3l, au_ck, lw("fm32")],
                        tag=f"f3u{it}", scr=rh2k[:, :] if FP64_SIM else None)
        A("vector", lambda e: e.tensor_tensor(fbf[:, :], fi32[:, :], fm32[:, :], AO.mult),
          waits=[rck1, rck2, lw("fbf")], inc=cv, tag=f"f3p{it}")
        pck = cv.cp()
        last["fi32"] = pck
        last["fm32"] = pck
        last["fl0"] = rck1
        last["fl1"] = rck2
        mi = mxi[0]; mxi[0] += 1
        A("vector", lambda e, mi=mi: e.tensor_reduce(mxcols[:, mi:mi + 1], fbf[:, :], AX.X, AO.max,
                                                     apply_absolute_value=True),
          waits=[pck], inc=cv, tag=f"f3x{it}")
        A("gpsimd", lambda e, it=it: e.dma_start(out=o_d.ap()[it * 128:(it + 1) * 128, :], in_=fbf[:, :]),
          waits=[cv.cp()], inc=ds_fbf, inc_n=16, tag=f"f3w{it}")
        last["fbf"] = ds_fbf
    landmarks["f3"] = len(s.ops)
    f3_end = cv.cp()
    local_finalize([(0, 24)], "r5x")
    r5 = ar_and_bcast("o", [5], "r5")
    ao_ck = recip_scale(5, 5, QM, [r5], "ao")
    last["oqc_g"] = f3_end

    if gm_dump is not None:
        A("gpsimd", lambda e: e.dma_start(out=gm_dump, in_=gmax[:, :]), waits=[ao_ck], inc=dsq, inc_n=16)
    for dpx_t, src_t in ((up_dump, up_d), (gt_dump, gate_d), (pd_dump, p_d), (od_dump, o_d)):
        if dpx_t is not None:
            A("gpsimd", lambda e, dpx_t=dpx_t, src_t=src_t: e.dma_start(out=dpx_t, in_=src_t.ap()),
              waits=[ao_ck, (ds_fbf, ds_fbf.val)] + [(c, c.val) for c in ds_fo], inc=dsq, inc_n=16)

    for kc in range(24):
        for r in range(8):
            A("sync", lambda e, kc=kc, r=r: e.dma_start(
                out=wdT[kc][:, r * 128:(r + 1) * 128],
                in_=ag_down_out.ap()[r * NI + kc * 128: r * NI + (kc + 1) * 128, :]),
              waits=[(ccd, 1), lw("wdt_g") if kc == 0 and r == 0 else None], inc=dwd, inc_n=16, tag=f"wdl{kc}")

    for tg in range(16):
        for it in range(24):
            A("scalar", lambda e, it=it, tg=tg: e.dma_start(
                out=fload[0][:, 0:128], in_=o_d.ap()[it * 128:(it + 1) * 128, tg * 128:(tg + 1) * 128]),
              waits=[lw("fl0"), (ds_fbf, ds_fbf.val) if tg == 0 and it == 0 else None], inc=dsc, inc_n=16,
              tag=f"f4l{tg}_{it}")
            rck = round_ts(oqi[:, :], fload[0][:, 0:128], gmax[:, 5:6], [(dsc, dsc.val), ao_ck, lw("oqi")],
                           tag=f"f4r{tg}_{it}", scr=fsig[:, 0:128] if FP64_SIM else None)
            last["fl0"] = rck
            A("vector", lambda e, it=it: e.tensor_copy(oqc[it][:, :], oqi[:, :]),
              waits=[rck, lw(f"oqc{it}"), lw("oqc_g") if tg == 0 and it == 0 else None], inc=cv, tag=f"f4q{tg}_{it}")
            last["oqi"] = cv.cp()
            last[f"oqc{it}"] = cv.cp()
        oq_ck = cv.cp()
        pb, pbn = (psA, "psA") if tg % 2 == 0 else (psB, "psB")
        for eg in range(2):
            for it in range(24):
                A("tensor", lambda e, it=it, eg=eg, pb=pb: e.matmul(
                    pb[:, eg * 512:(eg + 1) * 512], oqc[it][:, :],
                    wdT[it][:, eg * 512:(eg + 1) * 512], start=(it == 0), stop=(it == 23)),
                  waits=[oq_ck, (dwd, dwd.val), lw(pbn)], inc=ct, tag=f"f4m{tg}_{eg}")
        f4m = ct.cp()
        for itx in range(24):
            last[f"oqc{itx}"] = f4m
        ti = tg
        ck4 = ln_stats_apply(pb[:, :], ln_scr[:, :], waits=[f4m, lw("ln_scr")], tag=f"l4_{ti}")
        xr = xt[ti % 2]
        A("sync", lambda e, ti=ti, xr=xr: e.dma_start(out=xr[:, :], in_=x2_d.ap()[ti * 128:(ti + 1) * 128, :]),
          waits=[(ds2, ds2.val) if ti == 0 else None, f3_end if ti < 2 else None, lw(f"xt{ti % 2}")],
          inc=dxt[ti % 2], inc_n=16, tag=f"f4x{ti}")
        A("vector", lambda e, xr=xr: e.tensor_tensor(x2t[:, :], ln_scr[:, :], xr[:, :], AO.add),
          waits=[ck4, dxt[ti % 2].cp(), lw("x2t")], inc=cv, tag=f"f4a{ti}")
        last["ln_scr"] = cv.cp()
        last[f"xt{ti % 2}"] = cv.cp()
        last[pbn] = ck4
        A("gpsimd", lambda e, ti=ti: e.dma_start(out=out[ti * 128:(ti + 1) * 128, :], in_=x2t[:, :]),
          waits=[cv.cp()], inc=ds_out, inc_n=16, tag=f"OUT{ti}")
        last["x2t"] = ds_out

    landmarks["f4"] = len(s.ops)
    A("sync", lambda e: e.nop(),
      waits=[ds_out, dst, ds_st, ds2, ds3[0], ds3[1], ds_fbf, dsxl[0], dsxl[1], dsy, dsq,
             dpx, dxt[0], dxt[1], dx3[0], dx3[1], dwv, dwor, dwd, dbn, dagb[0], dagb[1],
             dscf[0], dscf[1], dsc, dwA, dwB, dcc, (ccu, 1), (ccg, 1), (ccd, 1), (ccs, 1),
             cv, ca, ct, cg] + [(c, c.val) for c in ds_fo],
      tag="FIN")
    build.landmarks = dict(landmarks)
    build.tags = [(o.eng, o.tag) for o in s.ops]
    if max_ops is not None:
        if isinstance(max_ops, str):
            max_ops = landmarks[max_ops]
        s.ops = s.ops[:max_ops]
    if os.environ.get("PE_ONLY"):
        lo = int(os.environ.get("PE_LO", "0"))
        kept = [o for o in s.ops if o.eng == "tensor"]
        kept = kept[:lo] if lo < 0 else kept[lo:]
        s.ops = [Op(o.eng, o.fn, [], o.inc, o.tag) for o in kept]
    s.emit()
    return nc, sorted(dump_aps.keys())


_NC_CACHE = {}

def _get_nc(dumps=()):
    key = tuple(sorted(dumps))
    if key not in _NC_CACHE:
        _NC_CACHE[key] = build(dumps)
    return _NC_CACHE[key]


def kernel(L0_attn_x, L0_attn_Wq, L0_attn_Wk, L0_attn_Wv, L0_attn_Wg, L0_attn_Wo,
           L0_attn_gamma, L0_attn_beta, L0_ffn_w_gate, L0_ffn_w_up, L0_ffn_w_down,
           _dumps=()):
    x = np.ascontiguousarray(np.asarray(L0_attn_x, dtype=np.float32))
    ws = {nm: np.ascontiguousarray(np.asarray(w, np.float32)) for nm, w in
          (("q", L0_attn_Wq), ("k", L0_attn_Wk), ("v", L0_attn_Wv), ("g", L0_attn_Wg), ("o", L0_attn_Wo))}
    gamma = np.asarray(L0_attn_gamma, np.float32).reshape(1, D)
    beta = np.asarray(L0_attn_beta, np.float32).reshape(1, D)
    wup = np.ascontiguousarray(np.asarray(L0_ffn_w_up, np.float32))
    wgate = np.ascontiguousarray(np.asarray(L0_ffn_w_gate, np.float32))
    wdown = np.ascontiguousarray(np.asarray(L0_ffn_w_down, np.float32))

    nc, dump_names = _get_nc(_dumps)
    in_maps = []
    for c in range(8):
        b, p = c // 2, c % 2
        psel = np.zeros((128, 8), np.float32)
        if p == 1:
            psel[:, c - 1] = 1.0
        in_maps.append({
            "x": np.ascontiguousarray(x[b, p * TC:(p + 1) * TC, :]),
            "wq": ws["q"], "wk": ws["k"], "wv": ws["v"], "wg": ws["g"], "wo": ws["o"],
            "gamma": gamma, "beta": beta,
            "wup_s": np.ascontiguousarray(wup[c * (NI // 8):(c + 1) * (NI // 8), :]),
            "wgate_s": np.ascontiguousarray(wgate[c * (NI // 8):(c + 1) * (NI // 8), :]),
            "wdown_s": np.ascontiguousarray(wdown[c * (D // 8):(c + 1) * (D // 8), :]),
            "psel": psel,
        })
    res = run_bass_kernel_spmd(nc, in_maps, core_ids=list(range(8)))
    outp = np.empty((4, 4096, D), np.float32)
    for c in range(8):
        b, p = c // 2, c % 2
        outp[b, p * TC:(p + 1) * TC, :] = np.asarray(res.results[c]["out"])
    if _dumps:
        return outp, [{k: np.asarray(res.results[c]["dump_" + k]) for k in _dumps} for c in range(8)]
    return outp
